# version 04 (final)
# baseline (speedup 1.0000x reference)
import math
import contextlib
import numpy as np
import ml_dtypes

import concourse.bass as bass
import concourse.mybir as mybir
from concourse.bass import AP
from concourse.bass_utils import run_bass_kernel_spmd

F32 = mybir.dt.float32
BF16 = mybir.dt.bfloat16
AF = mybir.ActivationFunctionType
ALU = mybir.AluOpType
AX = mybir.AxisListType

PI = float(np.pi)
PI_SAFE = float(np.nextafter(np.float32(np.pi), np.float32(0.0)))
TWO_PI = float(2 * np.pi)
ATT_SCALE = float(1.0 / math.sqrt(128.0))
LN_EPS = 1e-5

H = 2048
S = 2048
TOK = 1024
NHEAD = 16
NKV = 4
E = 8
F = 1408
FS = 2816
NSLOT = 6
PHASE_LIMIT = [99]
DEBUG_SPILL = [False]


class Ctx:

    def __init__(self, counts, labels, handles=None, eng_name=None, eng=None):
        self.counts = counts
        self.labels = labels
        self.handles = handles
        self.eng_name = eng_name
        self.eng = eng

    def alias(self, new, old):
        if self.eng_name is None:
            assert new not in self.labels, f"dup label {new}"
            self.labels[new] = self.labels[old]

    def __init_spacers__(self, spacers):
        self.spacers = spacers

    def do(self, eng, f=None, waits=(), sig=None, k=1, sem=None):
        sname0 = sem or eng
        if self.eng_name is None:
            if sig is not None:
                self.counts[sname0] += k
                assert sig not in self.labels, f"dup label {sig}"
                self.labels[sig] = (sname0, self.counts[sname0])
            return
        if eng != self.eng_name:
            return
        e = self.eng
        for w in waits:
            sname, val = self.labels[w]
            e.wait_ge(self.handles[sname], val)
        inst = f(e) if f is not None else None
        if sig is not None:
            sp = getattr(self, "spacers", {}).get(eng)
            if inst is None and sp is None:
                inst = e.nop()
            elif sp is not None and f is not None:
                inst = sp(e)
            elif inst is None:
                inst = sp(e)
            inst.then_inc(self.handles[sname0], k)


def build_program():
    nc = bass.Bass("TRN2", target_bir_lowering=False, detect_race_conditions=False)

    d_x = nc.dram_tensor("x_full", [S, H], F32, kind="ExternalInput")
    d_ksin = nc.dram_tensor("ksin", [64, S], F32, kind="ExternalInput")
    d_kcos = nc.dram_tensor("kcos", [64, S], F32, kind="ExternalInput")
    d_t0 = nc.dram_tensor("t0", [1, 1], F32, kind="ExternalInput")
    d_idb = nc.dram_tensor("id_bf", [128, 128], BF16, kind="ExternalInput")
    d_idf = nc.dram_tensor("id_f32", [128, 128], F32, kind="ExternalInput")
    d_wk = nc.dram_tensor("wk_t", [512, H], F32, kind="ExternalInput")
    d_wv = nc.dram_tensor("wv_t", [512, H], F32, kind="ExternalInput")
    d_wq = nc.dram_tensor("wq_t", [H, H], F32, kind="ExternalInput")
    d_wo = nc.dram_tensor("wo_t", [H, H], F32, kind="ExternalInput")
    d_wg = nc.dram_tensor("wg_t", [H, E], F32, kind="ExternalInput")
    d_sgu = nc.dram_tensor("sgu_t", [2 * FS, H], F32, kind="ExternalInput")
    d_sd = nc.dram_tensor("sd_t", [H, FS], F32, kind="ExternalInput")
    d_gu = nc.dram_tensor("gu_t", [E * 2 * F, H], F32, kind="ExternalInput")
    d_dn = nc.dram_tensor("dn_t", [E * H, F], F32, kind="ExternalInput")
    d_out = nc.dram_tensor("out", [TOK, H], F32, kind="ExternalOutput")
    d_sp = nc.dram_tensor("x1_spill", [TOK, H], F32,
                          kind="ExternalOutput" if DEBUG_SPILL[0] else "Internal")
    if DEBUG_SPILL[0]:
        d_dbg = nc.dram_tensor("dbg", [128, 4096], F32, kind="ExternalOutput")
        d_dbg2 = nc.dram_tensor("dbg2", [128, 32768], F32, kind="ExternalOutput")

    es = contextlib.ExitStack()

    def sb(name, shape, dt=F32):
        return es.enter_context(nc.sbuf_tensor(name, shape, dt))

    a1 = nc.sbuf_base
    xfm = sb("xfm", [128, 16 * S], BF16)
    end_a1 = nc.sbuf_base
    nc.sbuf_base = a1
    pp = sb("pp", [128, 16 * 1024], BF16)
    rb = sb("rb", [128, 1024], F32)
    qsq = sb("qsq", [128, S], F32)
    assert nc.sbuf_base <= end_a1
    nc.sbuf_base = a1
    zall = sb("zall", [128, 8 * 1024], F32)
    x1t = sb("x1t", [128, H], F32)
    x1b = sb("x1b", [128, H], BF16)
    xrh0 = sb("xrh0", [128, 1024], F32)
    xrh1 = sb("xrh1", [128, 1024], F32)
    junk = sb("junk", [128, 1024], F32)
    outt = sb("outt", [128, H], F32)
    assert nc.sbuf_base <= end_a1
    nc.sbuf_base = a1
    actA = sb("actA", [128, 11 * 1024], BF16)
    actB = sb("actB", [128, 11 * 1024], BF16)
    silt = sb("silt", [128, 1024], F32)
    tmpu = sb("tmpu", [128, 1024], F32)
    cbcA = sb("cbcA", [128, 1024], F32)
    cbcB = sb("cbcB", [128, 1024], F32)
    assert nc.sbuf_base <= end_a1
    nc.sbuf_base = end_a1

    a2 = nc.sbuf_base
    posk = sb("posk", [64, S], F32)
    fq = sb("fq", [64, S], F32)
    th1 = sb("th1", [64, S], F32)
    th2 = sb("th2", [64, S], F32)
    end_a2 = nc.sbuf_base
    nc.sbuf_base = a2
    qfm = [sb(f"qfm{h}", [128, TOK], BF16) for h in range(NHEAD)]
    assert nc.sbuf_base <= end_a2
    nc.sbuf_base = a2
    x1fm = sb("x1fm", [128, 16 * TOK], BF16)
    assert nc.sbuf_base <= end_a2
    nc.sbuf_base = end_a2

    a3 = nc.sbuf_base
    kfm = [sb(f"kfm{g}", [128, S], BF16) for g in range(NKV)]
    vtm = sb("vtm", [128, 16 * 512], BF16)
    a3r = nc.sbuf_base
    cosd = sb("cosd", [128, S], F32)
    sins = sb("sins", [128, S], F32)
    a3v = nc.sbuf_base
    ksin_s = sb("ksin_s", [64, S], F32)
    kcos_s = sb("kcos_s", [64, S], F32)
    end_a3 = nc.sbuf_base
    nc.sbuf_base = a3v
    vfm = [sb(f"vfm{g}", [128, S], BF16) for g in range(NKV)]
    assert nc.sbuf_base <= end_a3
    nc.sbuf_base = a3r
    aout = [sb(f"aout{h}", [128, TOK], BF16) for h in range(NHEAD)]
    assert nc.sbuf_base <= end_a3
    nc.sbuf_base = a3
    wores = sb("wores", [128, 16 * 1024], BF16)
    assert nc.sbuf_base <= end_a3
    nc.sbuf_base = a3
    yfm = sb("yfm", [128, 16 * 1024], F32)
    assert nc.sbuf_base <= end_a3
    nc.sbuf_base = end_a3

    wsl = [sb(f"wsl{i}", [128, H], BF16) for i in range(NSLOT)]

    idb = sb("idb", [128, 128], BF16)
    idf = sb("idf", [128, 128], F32)
    t0s = sb("t0s", [1, 1], F32)
    t0b = sb("t0b", [64, 1], F32)
    nt0b = sb("nt0b", [64, 1], F32)
    ipx = sb("ipx", [64, 1], F32)
    invf = sb("invf", [64, 1], F32)
    ones_b = sb("ones_b", [128, 1], BF16)
    ones_f = sb("ones_f", [1, 128], F32)
    onescol = sb("onescol", [128, 1], F32)
    epsc = sb("epsc", [128, 1], F32)
    negc = sb("negc", [128, NHEAD], F32)
    nkrow = sb("nkrow", [1, 16], F32)
    nqrow = sb("nqrow", [1, 16], F32)
    srow = sb("srow", [1, 1024], F32)
    rrow = sb("rrow", [1, 1024], F32)
    cmbr = sb("cmbr", [1, 1024], F32)
    ctm = [sb(f"ctm{t}", [128, E], F32) for t in range(8)]
    wgs = sb("wgs", [128, 16 * E], BF16)
    ropa = sb("ropa", [128, 512], F32)
    ropb = sb("ropb", [128, 512], F32)
    lns = sb("lns", [128, 8], F32)
    gsc = sb("gsc", [128, E], F32)
    gsc2 = sb("gsc2", [128, E], F32)
    tiny = sb("tiny", [1, 8], F32)
    vj = sb("vj", [128, 8], F32)
    sjv = sb("sjv", [128, 2], F32)

    assert nc.sbuf_base <= nc.sbuf_top, (nc.sbuf_base, nc.sbuf_top)

    ps12 = es.enter_context(nc.psum_tensor("ps12", [128, 2048], F32))
    p3 = es.enter_context(nc.psum_tensor("p3", [128, 1024], F32))
    pb = nc.psum_base
    p4 = es.enter_context(nc.psum_tensor("p4", [128, 512], F32))
    p5 = es.enter_context(nc.psum_tensor("p5", [128, 512], F32))
    end_ps = nc.psum_base
    nc.psum_base = pb
    pt = es.enter_context(nc.psum_tensor("pt", [128, 512], BF16))
    nc.psum_base = end_ps
    assert nc.psum_base <= nc.psum_top

    def q12(i):
        return ps12[:, 512 * i:512 * (i + 1)]

    stream = []

    def add_blk(tag, dram, r0, c0, ncols):
        stream.append((tag, dram, r0, c0, ncols))

    for i in range(16):
        add_blk(f"x{i}", d_x, i * 128, 0, H)
    for m in range(4):
        add_blk(f"wk{m}", d_wk, m * 128, 0, H)
    for m in range(4):
        add_blk(f"wv{m}", d_wv, m * 128, 0, H)
    for m in range(16):
        add_blk(f"wq{m}", d_wq, m * 128, 0, H)
    STREAM1_LEN = len(stream)
    sgu_order = [x for i in range(22) for x in (i, 22 + i)]
    for j, m in enumerate(sgu_order):
        add_blk(f"sgu{j}", d_sgu, m * 128, 0, H)
    for m in range(16):
        add_blk(f"sdA{m}", d_sd, m * 128, 0, FS // 2)
        add_blk(f"sdB{m}", d_sd, m * 128, FS // 2, FS // 2)
    gu_order = [x for i in range(11) for x in (i, 11 + i)]
    for e in range(E):
        for j, m in enumerate(gu_order):
            add_blk(f"gu{e}_{j}", d_gu, e * 2 * F + m * 128, 0, H)
        if e % 2 == 1:
            for m in range(16):
                for ee in (e - 1, e):
                    add_blk(f"dn{ee}_{m}", d_dn, ee * H + m * 128, 0, F)

    blk_index = {b[0]: i for i, b in enumerate(stream)}

    def slot_of(tag):
        return wsl[blk_index[tag] % NSLOT]

    counts = {"tensor": 0, "vector": 0, "scalar": 0, "gpsimd": 0, "sync": 0,
              "gcomp": 0, "gwo": 0, "gwg": 0, "sxr0": 0, "sxr1": 0,
              "sspw": 0, "sspr": 0, "sout": 0, "sld0": 0, "sld1": 0,
              "sld2": 0, "sld3": 0, "sld4": 0}
    for i in range(NSLOT):
        counts[f"slot{i}"] = 0
    labels = {}

    def walk(c: Ctx):
        T, V, Sc, G, Y = "tensor", "vector", "scalar", "gpsimd", "sync"
        LIM = PHASE_LIMIT[0]

        def truncate():
            c.do(Y, lambda e: e.dma_start(d_out[0:128, :], outt[:, :]),
                 sig="truncout", k=16, sem="sout")
            c.do(Y, None, waits=("truncout",))

        def emit_dma(bi):
            tag, dram, r0, c0, ncols = stream[bi]
            slot = wsl[bi % NSLOT]
            waits = []
            if bi >= NSLOT:
                waits.append(f"use_{stream[bi - NSLOT][0]}")
            c.do(G, lambda e, s=slot, d=dram, r0=r0, c0=c0, nn=ncols:
                 e.dma_start(s[:, 0:nn], d[r0:r0 + 128, c0:c0 + nn]),
                 waits=tuple(waits), sig=f"dma_{tag}", k=16,
                 sem=f"slot{bi % NSLOT}")

        nxt = [0]

        def consume():
            if nxt[0] < len(stream):
                emit_dma(nxt[0])
                nxt[0] += 1

        c.do(Y, lambda e: e.dma_start(idb[:, :], d_idb[:, :]), sig="ld_idb", k=16, sem="sld0")
        c.do(Y, lambda e: e.dma_start(idf[:, :], d_idf[:, :]), sig="ld_idf", k=16, sem="sld1")
        c.do(Y, lambda e: e.dma_start(ksin_s[:, :], d_ksin[:, :]), sig="ld_ks", k=16, sem="sld2")
        c.do(Y, lambda e: e.dma_start(kcos_s[:, :], d_kcos[:, :]), sig="ld_kc", k=16, sem="sld3")
        c.do(Y, lambda e: e.dma_start(t0s[:, :], d_t0[:, :]), sig="ld_t0", k=16, sem="sld4")
        c.do(G, lambda e: e.dma_start(
            AP(wgs, 0, [[128, 128], [8, 16], [1, 8]]),
            AP(d_wg, 0, [[8, 128], [1024, 16], [1, 8]])), sig="ld_wg", k=16, sem="gwg")

        def p0_g(e):
            e.iota(posk[:, :], [[1, S]], channel_multiplier=0,
                   allow_small_or_imprecise_dtypes=True)
            e.iota(ipx[:, :], [[0, 1]], channel_multiplier=1,
                   allow_small_or_imprecise_dtypes=True)
            e.memset(ones_b[:, :], 1.0)
            e.memset(ones_f[:, :], 1.0)
            e.memset(onescol[:, :], 1.0)
            e.memset(epsc[:, :], LN_EPS)
            e.memset(vj[:, :], 0.0)
            e.memset(sjv[:, :], 0.0)
            return e.drain()
        c.do(G, p0_g, sig="g_init", sem="gcomp")

        for bi in range(NSLOT):
            consume()

        c.do(T, lambda e: e.matmul(p5[0:64, 0:1], ones_f[0:1, 0:64], t0s[:, :],
                                   start=True, stop=True),
             waits=("ld_t0", "g_init"), sig="t0bc")
        c.do(V, lambda e: (e.tensor_copy(t0b[:, :], p5[0:64, 0:1]),
                           e.tensor_copy(vj[:, 0:4], vj[:, 4:8]),
                           e.tensor_scalar(nt0b[:, :], t0b[:, :], -1.0, None,
                                           ALU.mult))[-1],
             waits=("t0bc",), sig="t0done")
        c.do(V, lambda e: e.tensor_scalar(ipx[:, :], ipx[:, :],
                                          float(-np.log(10000.0) / 64.0), None,
                                          ALU.mult),
             waits=("g_init",), sig="ie")
        c.do(Sc, lambda e: e.activation(invf[:, :], ipx[:, :], AF.Exp),
             waits=("ie",), sig="invf")

        def p0_ang(e):
            e.tensor_scalar(posk[:, 0:TOK], posk[:, 0:TOK], t0b[:, :], None, ALU.add)
            e.tensor_scalar(posk[:, TOK:S], posk[:, TOK:S], nt0b[:, :], None, ALU.add)
            e.tensor_scalar(fq[:, :], posk[:, :], invf[:, :], None, ALU.mult)
            e.scalar_tensor_tensor(th1[:, :], ksin_s[:, :], -TWO_PI, fq[:, :],
                                   ALU.mult, ALU.add)
            e.tensor_scalar(th1[:, :], th1[:, :], -PI_SAFE, None, ALU.max)
            e.tensor_scalar(th1[:, :], th1[:, :], PI_SAFE, None, ALU.min)
            e.tensor_scalar(th2[:, :], fq[:, :], PI / 2.0, None, ALU.add)
            e.scalar_tensor_tensor(th2[:, :], kcos_s[:, :], -TWO_PI, th2[:, :],
                                   ALU.mult, ALU.add)
            e.tensor_scalar(th2[:, :], th2[:, :], -PI_SAFE, None, ALU.max)
            return e.tensor_scalar(th2[:, :], th2[:, :], PI_SAFE, None, ALU.min)
        c.do(V, p0_ang, waits=("t0done", "invf", "ld_ks", "ld_kc"), sig="angles")
        c.do(Sc, lambda e: (e.activation(th1[:, :], th1[:, :], AF.Sin),
                            e.activation(th2[:, :], th2[:, :], AF.Sin))[-1],
             waits=("angles",), sig="raws")

        def p0_tab(e):
            e.tensor_copy(cosd[0:64, :], th2[:, :])
            e.tensor_copy(cosd[64:128, :], th2[:, :])
            e.tensor_copy(sins[64:128, :], th1[:, :])
            return e.tensor_scalar(sins[0:64, :], th1[:, :], -1.0, None, ALU.mult)
        c.do(V, p0_tab, waits=("raws",), sig="tables")

        for i in range(16):
            for j in range(4):
                def f_tr(e, i=i, j=j):
                    last = None
                    for u in range(4):
                        hc = j * 4 + u
                        last = e.transpose(pt[:, 128 * u:128 * (u + 1)],
                                           slot_of(f"x{i}")[:, 128 * hc:128 * (hc + 1)],
                                           idb[:, :])
                    return last
                waits = [f"dma_x{i}", "ld_idb"]
                if (i, j) != (0, 0):
                    pi, pj = (i, j - 1) if j > 0 else (i - 1, 3)
                    waits.append(f"xe{pi}_{pj}")
                if i == 0 and j == 0:
                    waits.append("t0done")
                sig = f"xt{i}_{j}" if j < 3 else f"use_x{i}"
                c.do(T, f_tr, waits=tuple(waits), sig=sig)
                if j == 3:
                    consume()

                def f_ev(e, i=i, j=j):
                    dst = AP(xfm, (4 * j) * S + i * 128,
                             [[16 * S, 128], [S, 4], [1, 128]])
                    return e.tensor_copy(dst, pt[:, 0:512])
                wl = f"xt{i}_{j}" if j < 3 else f"use_x{i}"
                c.do(V, f_ev, waits=(wl,), sig=f"xe{i}_{j}")
        c.do(T, None, waits=("xe15_3",), sig="xdone")

        if LIM <= 0:
            truncate()
            return
        qctr = [0]

        def proj_group(epi_cb, lhs_tag, xoff, last_use):
            gi = qctr[0]
            qctr[0] += 1
            q = gi % 4

            def f_mm(e, gi=gi, q=q, lhs_tag=lhs_tag, xoff=xoff):
                last = None
                for k in range(16):
                    last = e.matmul(q12(q),
                                    slot_of(lhs_tag)[:, 128 * k:128 * (k + 1)],
                                    xfm[:, k * S + xoff:k * S + xoff + 512],
                                    start=(k == 0), stop=(k == 15))
                return last
            waits = [f"dma_{lhs_tag}", "xdone"]
            if gi >= 4:
                waits.append(f"pev{gi - 4}")
            sig = f"use_{lhs_tag}" if last_use else f"pmm{gi}"
            c.do(T, f_mm, waits=tuple(waits), sig=sig)
            if last_use:
                consume()
            epi_cb(q, sig, f"pev{gi}")

        def rope_ev(dstt, dst_off, tab_off):
            def cb(q, mmlabel, evlabel):
                def f_rope(e):
                    e.tensor_copy(ropb[0:64, :], q12(q)[64:128, :])
                    e.tensor_copy(ropb[64:128, :], q12(q)[0:64, :])
                    e.tensor_tensor(ropa[:, :], q12(q),
                                    cosd[:, tab_off:tab_off + 512], ALU.mult)
                    e.tensor_tensor(ropb[:, :], ropb[:, :],
                                    sins[:, tab_off:tab_off + 512], ALU.mult)
                    return e.tensor_tensor(dstt[:, dst_off:dst_off + 512],
                                           ropa[:, :], ropb[:, :], ALU.add)
                c.do(V, f_rope, waits=(mmlabel, "tables"), sig=evlabel)
            return cb

        def plain_ev(dstt, dst_off):
            def cb(q, mmlabel, evlabel):
                c.do(V, lambda e: e.tensor_copy(dstt[:, dst_off:dst_off + 512],
                                                q12(q)),
                     waits=(mmlabel,), sig=evlabel)
            return cb

        for m in range(4):
            for n in range(4):
                proj_group(rope_ev(kfm[m], n * 512, n * 512), f"wk{m}",
                           n * 512, last_use=(n == 3))
        for m in range(4):
            for n in range(4):
                proj_group(plain_ev(vfm[m], n * 512), f"wv{m}",
                           n * 512, last_use=(n == 3))
        for m in range(16):
            for n in range(2):
                proj_group(rope_ev(qfm[m], n * 512, n * 512), f"wq{m}",
                           n * 512, last_use=(n == 1))

        for g in range(4):
            for j in range(4):
                def f_vt(e, g=g, j=j):
                    last = None
                    for u in range(4):
                        kc = j * 4 + u
                        last = e.transpose(pt[:, 128 * u:128 * (u + 1)],
                                           vfm[g][:, 128 * kc:128 * (kc + 1)],
                                           idb[:, :])
                    return last
                waits = [f"pev{16 + g * 4 + 3}"]
                if (g, j) != (0, 0):
                    pg, pj = (g, j - 1) if j > 0 else (g - 1, 3)
                    waits.append(f"vte{pg}_{pj}")
                c.do(T, f_vt, waits=tuple(waits), sig=f"vtt{g}_{j}")

                def f_vte(e, g=g, j=j):
                    dst = AP(vtm, (4 * j) * 512 + g * 128,
                             [[16 * 512, 128], [512, 4], [1, 128]])
                    return e.tensor_copy(dst, pt[:, 0:512])
                c.do(V, f_vte, waits=(f"vtt{g}_{j}",), sig=f"vte{g}_{j}")

        if LIM <= 1:
            truncate()
            return
        for g in range(4):
            c.do(Sc, lambda e, g=g: e.activation(qsq[:, :], kfm[g][:, :], AF.Square),
                 waits=(f"pev{g * 4 + 3}",) +
                       ((f"nk_done{g - 1}",) if g > 0 else ("vte3_3", "pev63")),
                 sig=f"ksq{g}")
            for cp in range(2):
                def f_nk(e, cp=cp):
                    e.matmul(p4[0:1, 0:512], onescol[:, 0:1],
                             qsq[:, 1024 * cp:1024 * cp + 512], start=True, stop=True)
                    return e.matmul(p5[0:1, 0:512], onescol[:, 0:1],
                                    qsq[:, 1024 * cp + 512:1024 * (cp + 1)],
                                    start=True, stop=True)
                c.do(T, f_nk, waits=(f"ksq{g}",) +
                     ((f"nkr{g}_{cp - 1}",) if cp else ()), sig=f"nkm{g}_{cp}")

                def f_nkr(e, g=g, cp=cp):
                    e.reduce_max(tiny[0:1, 0:1], p4[0:1, 0:512], axis=AX.X)
                    e.reduce_max(tiny[0:1, 1:2], p5[0:1, 0:512], axis=AX.X)
                    e.tensor_copy(vj[:, 0:4], vj[:, 4:8])
                    e.tensor_tensor(tiny[0:1, 2:3], tiny[0:1, 0:1],
                                    tiny[0:1, 1:2], ALU.max)
                    e.tensor_copy(vj[:, 0:4], vj[:, 4:8])
                    if cp == 0:
                        return e.tensor_copy(nkrow[0:1, g:g + 1], tiny[0:1, 2:3])
                    return e.tensor_tensor(nkrow[0:1, g:g + 1], nkrow[0:1, g:g + 1],
                                           tiny[0:1, 2:3], ALU.max)
                c.do(V, f_nkr, waits=(f"nkm{g}_{cp}",), sig=f"nkr{g}_{cp}")
            c.do(V, None, waits=(f"nkr{g}_1",), sig=f"nk_done{g}")
        for h in range(NHEAD):
            c.do(Sc, lambda e, h=h: e.activation(qsq[:, 0:1024], qfm[h][:, :],
                                                 AF.Square),
                 waits=((f"nq_done{h - 1}",) if h > 0 else ("nk_done3",)),
                 sig=f"qsq{h}")

            def f_nq(e):
                e.matmul(p4[0:1, 0:512], onescol[:, 0:1], qsq[:, 0:512],
                         start=True, stop=True)
                return e.matmul(p5[0:1, 0:512], onescol[:, 0:1], qsq[:, 512:1024],
                                start=True, stop=True)
            c.do(T, f_nq, waits=(f"qsq{h}",), sig=f"nqm{h}")

            def f_nqr(e, h=h):
                e.reduce_max(tiny[0:1, 0:1], p4[0:1, 0:512], axis=AX.X)
                e.reduce_max(tiny[0:1, 1:2], p5[0:1, 0:512], axis=AX.X)
                e.tensor_copy(vj[:, 0:4], vj[:, 4:8])
                e.tensor_tensor(nqrow[0:1, h:h + 1], tiny[0:1, 0:1],
                                tiny[0:1, 1:2], ALU.max)
                e.tensor_copy(vj[:, 0:4], vj[:, 4:8])
                return e.tensor_tensor(nqrow[0:1, h:h + 1], nqrow[0:1, h:h + 1],
                                       nkrow[0:1, h // 4:h // 4 + 1], ALU.mult)
            c.do(V, f_nqr, waits=(f"nqm{h}",), sig=f"nq_done{h}")
        for h in range(NHEAD):
            c.do(Sc, lambda e, h=h: e.activation(nqrow[0:1, h:h + 1],
                                                 nqrow[0:1, h:h + 1],
                                                 AF.Sqrt, scale=1.0 / 128.0),
                 waits=("nq_done15",) if h == 0 else (f"cbe{h - 1}",),
                 sig=f"csq{h}")
            c.do(V, lambda e, h=h: e.tensor_scalar(tiny[0:1, 5:6],
                                                   nqrow[0:1, h:h + 1],
                                                   -1.0, None, ALU.mult),
                 waits=(f"csq{h}",), sig=f"cneg{h}")
            c.do(T, lambda e: e.matmul(p4[:, 0:1], ones_f[0:1, :],
                                       tiny[0:1, 5:6], start=True, stop=True),
                 waits=(f"cneg{h}",), sig=f"cbm{h}")
            c.do(V, lambda e, h=h: e.tensor_copy(negc[:, h:h + 1], p4[:, 0:1]),
                 waits=(f"cbm{h}",), sig=f"cbe{h}")

        for h in range(NHEAD):
            g = h // 4
            for kc in range(16):
                half = kc % 2

                def f_st(e, h=h, kc=kc, half=half, g=g):
                    e.matmul(ps12[:, half * 1024:half * 1024 + 512],
                             kfm[g][:, 128 * kc:128 * (kc + 1)],
                             qfm[h][:, 0:512], start=True, stop=True)
                    return e.matmul(ps12[:, half * 1024 + 512:(half + 1) * 1024],
                                    kfm[g][:, 128 * kc:128 * (kc + 1)],
                                    qfm[h][:, 512:1024], start=True, stop=True)
                waits = []
                if h == 0 and kc == 0:
                    waits.append("cbe15")
                if kc >= 2:
                    waits.append(f"ex{h}_{kc - 2}")
                elif h > 0:
                    waits.append(f"ex{h - 1}_{14 + kc}")
                c.do(T, f_st, waits=tuple(waits), sig=f"st{h}_{kc}")

                ewaits = [f"st{h}_{kc}"]
                if h > 0 and kc == 0:
                    ewaits.append(f"pv{h - 1}")
                c.do(Sc, lambda e, h=h, kc=kc, half=half: e.activation(
                        pp[:, kc * 1024:(kc + 1) * 1024],
                        ps12[:, half * 1024:(half + 1) * 1024],
                        AF.Exp, bias=negc[:, h:h + 1], scale=ATT_SCALE),
                     waits=tuple(ewaits), sig=f"ex{h}_{kc}")

            def f_sp(e):
                last = None
                for kc in range(16):
                    e.matmul(p4[0:1, 0:512], ones_b[:, 0:1],
                             pp[:, kc * 1024:kc * 1024 + 512],
                             start=(kc == 0), stop=(kc == 15))
                    last = e.matmul(p5[0:1, 0:512], ones_b[:, 0:1],
                                    pp[:, kc * 1024 + 512:(kc + 1) * 1024],
                                    start=(kc == 0), stop=(kc == 15))
                return last
            spw = [f"ex{h}_15"]
            if h > 0:
                spw.append(f"rb{h - 1}")
            c.do(T, f_sp, waits=tuple(spw), sig=f"sp{h}")
            c.do(V, lambda e: (e.tensor_copy(srow[0:1, 0:512], p4[0:1, 0:512]),
                               e.tensor_copy(srow[0:1, 512:1024],
                                             p5[0:1, 0:512]))[-1],
                 waits=(f"sp{h}",), sig=f"ss{h}")
            c.do(V, lambda e: e.reciprocal(rrow[0:1, :], srow[0:1, :]),
                 waits=(f"ss{h}",), sig=f"rr{h}")

            def f_pv(e, g=g):
                last = None
                for kc in range(16):
                    e.matmul(p3[:, 0:512],
                             vtm[:, kc * 512 + g * 128:kc * 512 + (g + 1) * 128],
                             pp[:, kc * 1024:kc * 1024 + 512],
                             start=(kc == 0), stop=(kc == 15))
                    last = e.matmul(p3[:, 512:1024],
                                    vtm[:, kc * 512 + g * 128:kc * 512 + (g + 1) * 128],
                                    pp[:, kc * 1024 + 512:(kc + 1) * 1024],
                                    start=(kc == 0), stop=(kc == 15))
                return last
            c.do(T, f_pv, waits=(f"ao{h - 1}",) if h > 0 else (), sig=f"pv{h}")

            def f_rbc(e):
                e.matmul(p4[:, 0:512], ones_f[0:1, :], rrow[0:1, 0:512],
                         start=True, stop=True)
                return e.matmul(p5[:, 0:512], ones_f[0:1, :], rrow[0:1, 512:1024],
                                start=True, stop=True)
            c.do(T, f_rbc, waits=(f"rr{h}",), sig=f"rbm{h}")
            rbw = [f"rbm{h}"]
            if DEBUG_SPILL[0] and h == 1:
                rbw.append("dbgrr")
            c.do(V, lambda e: (e.tensor_copy(rb[:, 0:512], p4[:, :]),
                               e.tensor_copy(rb[:, 512:1024], p5[:, :]))[-1],
                 waits=tuple(rbw), sig=f"rb{h}")
            c.do(V, lambda e, h=h: e.tensor_tensor(aout[h][:, :], p3[:, :],
                                                   rb[:, :], ALU.mult),
                 waits=(f"pv{h}", f"rb{h}"), sig=f"ao{h}")
            if DEBUG_SPILL[0] and h == 0:
                c.do(Y, lambda e: e.dma_start(d_dbg2[:, 26624:27648], rb[:, 0:1024]),
                     waits=("ao0",), sig="dbgrb", k=16, sem="sspw")
                c.do(Y, lambda e: e.dma_start(d_dbg2[0:1, 27648:28672],
                                              rrow[:, 0:1024]),
                     sig="dbgrr", k=16, sem="sspw")

        if LIM <= 2:
            truncate()
            return
        if DEBUG_SPILL[0]:
            c.do(Y, lambda e: e.dma_start(d_dbg[:, 0:16], negc[:, :]),
                 waits=("ao15",), sig="dbga", k=16, sem="sspr")
            c.do(Y, lambda e: e.dma_start(d_dbg[0:1, 16:32], nqrow[:, :]),
                 sig="dbgb", k=16, sem="sspr")
            c.do(Y, lambda e: e.dma_start(d_dbg[0:1, 32:48], nkrow[:, :]),
                 sig="dbgc", k=16, sem="sspr")
            c.do(Y, lambda e: e.dma_start(d_dbg[0:1, 48:1072], srow[:, :]),
                 sig="dbgd", k=16, sem="sspr")
            c.do(Y, lambda e: e.dma_start(d_dbg[0:1, 1072:2096], rrow[:, :]),
                 sig="dbge", k=16, sem="sspr")
            c.do(Y, lambda e: e.dma_start(d_dbg[:, 2096:3120], rb[:, 0:1024]),
                 sig="dbgw", k=16, sem="sspr")
            for hh in range(16):
                c.do(G, lambda e, hh=hh: e.dma_start(
                        d_dbg2[:, hh * 1024:(hh + 1) * 1024], aout[hh][:, :]),
                     waits=("dbgw",) if hh == 0 else (),
                     sig=f"dbg2a{hh}", k=16, sem="gwg")
        for half in range(2):
            for khd in range(16):
                waits = []
                if half == 0 and khd == 0:
                    waits.append("pv15")
                if half == 1 and khd == 0:
                    waits.append("om0_7")
                c.do(G, lambda e, half=half, khd=khd: e.dma_start(
                        wores[:, khd * 1024:(khd + 1) * 1024],
                        d_wo[khd * 128:(khd + 1) * 128,
                             half * 1024:(half + 1) * 1024]),
                     waits=tuple(waits), sig=f"wo{half}_{khd}", k=16, sem="gwo")
        if DEBUG_SPILL[0]:
            c.do(G, lambda e: e.dma_start(d_dbg2[:, 24576:26624], wores[:, 0:2048]),
                 waits=("om1_7",), sig="dbg2w2", k=16, sem="gwg")
            c.do(G, lambda e: e.dma_start(d_dbg2[:, 16384:24576], zall[:, :]),
                 waits=("oz0_7",), sig="dbg2zl", k=16, sem="gwg")

        for half in range(2):
            xrh = (xrh0, xrh1)
            for tt in range(8):
                c.do(Y, lambda e, half=half, tt=tt: e.dma_start(
                        xrh[tt % 2][:, :],
                        d_x[tt * 128:(tt + 1) * 128,
                            half * 1024:(half + 1) * 1024]),
                     waits=(f"oz{half}_{tt - 2}",) if tt >= 2 else
                           ((f"oz0_{6 + tt}",) if half == 1 else ("pev63",)),
                     sig=f"xr{half}_{tt}", k=16, sem=f"sxr{tt % 2}")

                buf = tt % 2

                def f_o(e, half=half, tt=tt, buf=buf):
                    last = None
                    for hc in range(2):
                        for khd in range(16):
                            last = e.matmul(
                                ps12[:, buf * 1024 + hc * 512:buf * 1024 + hc * 512 + 512],
                                aout[khd][:, tt * 128:(tt + 1) * 128],
                                wores[:, khd * 1024 + hc * 512:khd * 1024 + hc * 512 + 512],
                                start=(khd == 0), stop=(khd == 15))
                    return last
                ow = [f"wo{half}_15", "ao15"]
                if tt >= 2:
                    ow.append(f"oz{half}_{tt - 2}")
                elif half == 1:
                    ow.append(f"oz0_{6 + tt}")
                c.do(T, f_o, waits=tuple(ow), sig=f"om{half}_{tt}")

                if half == 0:
                    c.do(V, lambda e, tt=tt, buf=buf: e.tensor_tensor(
                            zall[:, tt * 1024:(tt + 1) * 1024],
                            ps12[:, buf * 1024:(buf + 1) * 1024],
                            xrh[tt % 2][:, :], ALU.add),
                         waits=(f"om0_{tt}", f"xr0_{tt}"), sig=f"oz0_{tt}")
                else:
                    def f_z1(e, tt=tt, buf=buf):
                        e.tensor_tensor(x1t[:, 1024:2048],
                                        ps12[:, buf * 1024:(buf + 1) * 1024],
                                        xrh[tt % 2][:, :], ALU.add)
                        e.reduce_sum(lns[:, 0:1], zall[:, tt * 1024:(tt + 1) * 1024],
                                     axis=AX.X)
                        return e.reduce_sum(lns[:, 1:2], x1t[:, 1024:2048], axis=AX.X)
                    c.do(V, f_z1, waits=(f"om1_{tt}", f"xr1_{tt}") +
                         ((f"spw{tt - 1}",) if tt > 0 else ()), sig=f"oz1_{tt}")

                    def f_sq(e, tt=tt):
                        e.activation(junk[:, :], zall[:, tt * 1024:(tt + 1) * 1024],
                                     AF.Square, accum_out=lns[:, 2:3])
                        return e.activation(junk[:, :], x1t[:, 1024:2048],
                                            AF.Square, accum_out=lns[:, 3:4])
                    c.do(Sc, f_sq, waits=(f"oz1_{tt}",), sig=f"sq1_{tt}")

                    def f_st(e, tt=tt):
                        e.tensor_copy(vj[:, 0:4], vj[:, 4:8])
                        e.tensor_tensor(lns[:, 0:1], lns[:, 0:1], lns[:, 1:2], ALU.add)
                        e.tensor_tensor(lns[:, 2:3], lns[:, 2:3], lns[:, 3:4], ALU.add)
                        e.tensor_scalar(lns[:, 4:5], lns[:, 0:1], 1.0 / H, None,
                                        ALU.mult)
                        e.tensor_scalar(lns[:, 5:6], lns[:, 2:3], 1.0 / H, None,
                                        ALU.mult)
                        e.tensor_tensor(lns[:, 6:7], lns[:, 4:5], lns[:, 4:5],
                                        ALU.mult)
                        e.tensor_copy(vj[:, 0:4], vj[:, 4:8])
                        return e.tensor_tensor(lns[:, 5:6], lns[:, 5:6],
                                               lns[:, 6:7], ALU.subtract)
                    c.do(V, f_st, waits=(f"sq1_{tt}",), sig=f"lnsa{tt}")
                    c.do(Sc, lambda e, tt=tt: e.activation(lns[:, 7:8],
                                                           lns[:, 5:6], AF.Sqrt,
                                                           bias=epsc[:, :]),
                         waits=(f"lnsa{tt}",), sig=f"sd{tt}")

                    def f_norm(e, tt=tt):
                        e.reciprocal(lns[:, 6:7], lns[:, 7:8])
                        e.tensor_scalar(lns[:, 4:5], lns[:, 4:5], -1.0, None,
                                        ALU.mult)
                        e.tensor_copy(vj[:, 0:4], vj[:, 4:8])
                        e.tensor_scalar(x1t[:, 0:1024],
                                        zall[:, tt * 1024:(tt + 1) * 1024],
                                        lns[:, 4:5], lns[:, 6:7], ALU.add, ALU.mult)
                        e.tensor_scalar(x1t[:, 1024:2048], x1t[:, 1024:2048],
                                        lns[:, 4:5], lns[:, 6:7], ALU.add, ALU.mult)
                        return e.tensor_copy(x1b[:, :], x1t[:, :])
                    c.do(V, f_norm, waits=(f"sd{tt}",) +
                         ((f"x1t{tt - 1}_3",) if tt > 0 else ()), sig=f"x1n{tt}")
                    c.do(Y, lambda e, tt=tt: e.dma_start(
                            d_sp[tt * 128:(tt + 1) * 128, :], x1t[:, :]),
                         waits=(f"x1n{tt}",), sig=f"spw{tt}", k=16, sem="sspw")
                    for j in range(4):
                        def f_xt(e, tt=tt, j=j):
                            last = None
                            for u in range(4):
                                hc = j * 4 + u
                                last = e.transpose(pt[:, 128 * u:128 * (u + 1)],
                                                   x1b[:, 128 * hc:128 * (hc + 1)],
                                                   idb[:, :])
                            return last
                        xw = [f"x1n{tt}"]
                        if (tt, j) != (0, 0):
                            ptt, pj = (tt, j - 1) if j > 0 else (tt - 1, 3)
                            xw.append(f"x1e{ptt}_{pj}")
                        c.do(T, f_xt, waits=tuple(xw), sig=f"x1t{tt}_{j}")

                        def f_xe(e, tt=tt, j=j):
                            dst = AP(x1fm, (4 * j) * TOK + tt * 128,
                                     [[16 * TOK, 128], [TOK, 4], [1, 128]])
                            return e.tensor_copy(dst, pt[:, 0:512])
                        c.do(V, f_xe, waits=(f"x1t{tt}_{j}",), sig=f"x1e{tt}_{j}")
        c.do(T, None, waits=("x1e7_3",), sig="x1done")

        if LIM <= 3:
            truncate()
            return
        for tt in range(8):
            def f_lg(e, tt=tt):
                last = None
                for k in range(16):
                    last = e.matmul(p4[:, 0:8],
                                    x1fm[:, k * TOK + tt * 128:k * TOK + (tt + 1) * 128],
                                    wgs[:, k * 8:(k + 1) * 8],
                                    start=(k == 0), stop=(k == 15))
                return last
            lw = ["x1done", "ld_wg"]
            if tt > 0:
                lw.append(f"cmb{tt - 1}")
            c.do(T, f_lg, waits=tuple(lw), sig=f"lg{tt}")
            c.do(V, lambda e, tt=tt: e.reduce_max(lns[:, 0:1], p4[:, 0:8],
                                                  axis=AX.X, negate=True),
                 waits=(f"lg{tt}",), sig=f"gm{tt}")
            c.do(Sc, lambda e, tt=tt: e.activation(gsc[:, :], p4[:, 0:8], AF.Exp,
                                                   bias=lns[:, 0:1],
                                                   accum_out=lns[:, 1:2]),
                 waits=(f"gm{tt}",), sig=f"ge{tt}")

            def f_cmb(e, tt=tt):
                e.reciprocal(lns[:, 2:3], lns[:, 1:2])
                e.tensor_copy(vj[:, 0:4], vj[:, 4:8])
                e.tensor_scalar(gsc[:, :], gsc[:, :], lns[:, 2:3], None, ALU.mult)
                e.tensor_copy(vj[:, 0:4], vj[:, 4:8])
                e.reduce_max(lns[:, 3:4], gsc[:, :], axis=AX.X, negate=True)
                e.tensor_copy(vj[:, 0:4], vj[:, 4:8])
                e.tensor_scalar(gsc2[:, :], gsc[:, :], lns[:, 3:4], 0.0,
                                ALU.add, ALU.is_ge)
                e.tensor_copy(vj[:, 0:4], vj[:, 4:8])
                e.scalar_tensor_tensor(gsc2[:, :], gsc2[:, :], -1e30, gsc[:, :],
                                       ALU.mult, ALU.add)
                e.tensor_copy(vj[:, 0:4], vj[:, 4:8])
                e.reduce_max(lns[:, 4:5], gsc2[:, :], axis=AX.X, negate=True)
                e.tensor_copy(vj[:, 0:4], vj[:, 4:8])
                e.tensor_scalar(gsc2[:, :], gsc[:, :], lns[:, 4:5], 0.0,
                                ALU.add, ALU.is_ge)
                e.tensor_copy(vj[:, 0:4], vj[:, 4:8])
                return e.tensor_tensor(ctm[tt][:, :], gsc[:, :], gsc2[:, :],
                                       ALU.mult)
            c.do(V, f_cmb, waits=(f"ge{tt}",), sig=f"cmb{tt}")

        if LIM <= 4:
            truncate()
            return
        ffn_state = {"i": 0}

        def gu_pair(tag_gate, tag_up, act_dst, act_off, cbc, phase):
            i = ffn_state["i"]
            ffn_state["i"] += 1

            def f_gate(e, tag=tag_gate):
                last = None
                for n in range(2):
                    for k in range(16):
                        last = e.matmul(ps12[:, n * 512:(n + 1) * 512],
                                        slot_of(tag)[:, 128 * k:128 * (k + 1)],
                                        x1fm[:, k * TOK + n * 512:k * TOK + n * 512 + 512],
                                        start=(k == 0), stop=(k == 15))
                return last
            gw = [f"dma_{tag_gate}", "x1done"]
            if i > 0:
                gw.append(f"sil{i - 1}")
            else:
                gw.append("cmb7")
                gw.append("oz1_7")
            c.do(T, f_gate, waits=tuple(gw), sig=f"use_{tag_gate}")
            consume()

            def f_up(e, tag=tag_up):
                last = None
                for n in range(2):
                    for k in range(16):
                        last = e.matmul(ps12[:, 1024 + n * 512:1024 + (n + 1) * 512],
                                        slot_of(tag)[:, 128 * k:128 * (k + 1)],
                                        x1fm[:, k * TOK + n * 512:k * TOK + n * 512 + 512],
                                        start=(k == 0), stop=(k == 15))
                return last
            uw = [f"dma_{tag_up}"]
            if i > 0:
                uw.append(f"act{i - 1}")
            else:
                uw.append("oz1_7")
            c.do(T, f_up, waits=tuple(uw), sig=f"use_{tag_up}")
            consume()

            c.do(Sc, lambda e: e.activation(silt[:, :], ps12[:, 0:1024], AF.Silu),
                 waits=(f"use_{tag_gate}",), sig=f"sil{i}")
            if cbc is None:
                c.do(V, lambda e: e.tensor_tensor(act_dst[:, act_off:act_off + 1024],
                                                  silt[:, :], ps12[:, 1024:2048],
                                                  ALU.mult),
                     waits=(f"sil{i}", f"use_{tag_up}") +
                           (("ffn_safe",) if i == 0 else ()), sig=f"act{i}")
            else:
                def f_act(e):
                    e.tensor_tensor(tmpu[:, :], ps12[:, 1024:2048], cbc[:, :],
                                    ALU.mult)
                    return e.tensor_tensor(act_dst[:, act_off:act_off + 1024],
                                           silt[:, :], tmpu[:, :], ALU.mult)
                c.do(V, f_act, waits=(f"sil{i}", f"use_{tag_up}", f"cbc{cbc_id[0]}"),
                     sig=f"act{i}")
            return i

        c.do(V, None, waits=("spw7", "x1e7_3"), sig="ffn_safe")
        shared_acts = []
        NSH = 22 if LIM > 40 else 3
        for ii in range(NSH):
            dst, off = (actA, ii * 1024) if ii < 11 else (actB, (ii - 11) * 1024)
            shared_acts.append(gu_pair(f"sgu{2 * ii}", f"sgu{2 * ii + 1}",
                                       dst, off, None, "sh"))

        yround = [0]
        yround_start = [0]

        def down_round(act_srcs, first_write, round_tag):
            slots = [p3[:, 0:512], p3[:, 512:1024], p4[:, 0:512], p5[:, 0:512]]
            yround_start[0] = yround[0]
            for m in range(16):
                for n in range(2):
                    gi = yround[0]
                    yround[0] += 1
                    sl = slots[gi % 4]

                    def f_dn(e, m=m, n=n, sl=sl):
                        last = None
                        ops = []
                        for (tag_fn, act_t, act_base, nfc) in act_srcs:
                            for fc in range(nfc):
                                ops.append((tag_fn(m), fc, act_t, act_base))
                        nops = len(ops)
                        for oi, (tag, fc, act_t, act_base) in enumerate(ops):
                            last = e.matmul(
                                sl, slot_of(tag)[:, 128 * fc:128 * (fc + 1)],
                                act_t[:, (act_base + fc) * 1024 + n * 512:
                                      (act_base + fc) * 1024 + n * 512 + 512],
                                start=(oi == 0), stop=(oi == nops - 1))
                        return last
                    dw = [f"dma_{tf(m)}" for (tf, _, _, _) in act_srcs]
                    dw += [f"actrdy_{round_tag}"]
                    if gi >= 4:
                        dw.append(f"yad{gi - 4}")
                    if gi in (2, 3):
                        dw.append("ge7")
                    if round_tag != "sh" and gi % 4 in (2, 3) and gi - yround_start[0] < 4:
                        dw.append(f"cbc{2 * int(round_tag[1:]) + 1}")
                    c.do(T, f_dn, waits=tuple(dw),
                         sig=f"ydm{gi}" if n == 0 else f"yuse_{round_tag}_{m}")
                    if n == 1:
                        for (tag_fn, _, _, _) in act_srcs:
                            c.alias(f"use_{tag_fn(m)}", f"yuse_{round_tag}_{m}")
                            consume()

                    def f_ya(e, m=m, n=n, sl=sl, first=first_write):
                        dst = yfm[:, m * 1024 + n * 512:m * 1024 + n * 512 + 512]
                        if first:
                            return e.tensor_copy(dst, sl)
                        return e.tensor_tensor(dst, sl, dst, ALU.add)
                    c.do(V, f_ya,
                         waits=((f"ydm{gi}",) if n == 0 else
                                (f"yuse_{round_tag}_{m}",)),
                         sig=f"yad{gi}")

        if LIM <= 41:
            truncate()
            return
        c.do(V, None, waits=(f"act{shared_acts[-1]}",), sig="actrdy_sh")
        down_round([(lambda m: f"sdA{m}", actA, 0, 11),
                    (lambda m: f"sdB{m}", actB, 0, 11)],
                   True, "sh")

        if LIM <= 5:
            truncate()
            return
        cbc_id = [0]
        for pair in range(4):
            for sub in range(2):
                e_idx = 2 * pair + sub
                cbc = cbcA if sub == 0 else cbcB
                cid = e_idx
                cbc_id[0] = cid

                for tt in range(8):
                    if tt > 0:
                        ctw = [f"ctv{cid}_{tt - 1}"]
                    else:
                        ctw = [f"cmb{tt}", "ld_idf"]
                        if sub == 1:
                            ctw.append(f"cbc{cid - 1}")
                        else:
                            ctw.append(f"yad{yround[0] - 1}" if yround[0] > 0
                                       else "cmb7")
                    c.do(T, lambda e, tt=tt, e_idx=e_idx: e.transpose(
                            p5[0:1, 0:128], ctm[tt][:, e_idx:e_idx + 1], idf[:, :]),
                         waits=tuple(ctw), sig=f"ctt{cid}_{tt}")
                    c.do(V, lambda e, tt=tt: e.tensor_copy(
                            cmbr[0:1, tt * 128:(tt + 1) * 128], p5[0:1, 0:128]),
                         waits=(f"ctt{cid}_{tt}",), sig=f"ctv{cid}_{tt}")

                def f_cb(e):
                    e.matmul(p4[:, 0:512], ones_f[0:1, :], cmbr[0:1, 0:512],
                             start=True, stop=True)
                    return e.matmul(p5[:, 0:512], ones_f[0:1, :],
                                    cmbr[0:1, 512:1024], start=True, stop=True)
                cbw = [f"ctv{cid}_7"]
                if yround[0] > 0:
                    cbw.append(f"yad{yround[0] - 1}")
                c.do(T, f_cb, waits=tuple(cbw), sig=f"cbb{cid}")
                c.do(V, lambda e, cbc=cbc: (e.tensor_copy(cbc[:, 0:512], p4[:, :]),
                                            e.tensor_copy(cbc[:, 512:1024],
                                                          p5[:, :]))[-1],
                     waits=(f"cbb{cid}",), sig=f"cbc{cid}")

                act_dst = actA if sub == 0 else actB
                for ii in range(11):
                    gu_pair(f"gu{e_idx}_{2 * ii}", f"gu{e_idx}_{2 * ii + 1}",
                            act_dst, ii * 1024, cbc, f"e{e_idx}")
            c.do(V, None, waits=(f"act{ffn_state['i'] - 1}",),
                 sig=f"actrdy_p{pair}")
            labels_alias = f"actrdy_p{pair}"
            down_round([(lambda m, e0=2 * pair: f"dn{e0}_{m}", actA, 0, 11),
                        (lambda m, e1=2 * pair + 1: f"dn{e1}_{m}", actB, 0, 11)],
                       False, f"p{pair}")

        if LIM <= 6:
            truncate()
            return
        last_yad = yround[0] - 1
        for tt in range(8):
            c.do(Y, lambda e, tt=tt: e.dma_start(x1t[:, :],
                                                 d_sp[tt * 128:(tt + 1) * 128, :]),
                 waits=(f"spw7",) + ((f"o2n{tt - 1}",) if tt > 0 else
                                     ("yuse_p3_15",)),
                 sig=f"spr{tt}", k=16, sem="sspr")
            for j in range(4):
                def f_yt(e, tt=tt, j=j):
                    last = None
                    for u in range(4):
                        m = 4 * j + u
                        last = e.transpose(p4[:, 128 * u:128 * (u + 1)],
                                           yfm[:, m * 1024 + tt * 128:
                                               m * 1024 + (tt + 1) * 128],
                                           idf[:, :])
                    return last
                yw = []
                if tt == 0 and j == 0:
                    yw.append(f"yad{last_yad}")
                    yw.append(f"cbc7")
                else:
                    ptt, pj = (tt, j - 1) if j > 0 else (tt - 1, 3)
                    yw.append(f"yte{ptt}_{pj}")
                    if j == 0:
                        yw.append(f"z2{tt - 1}")
                c.do(T, f_yt, waits=tuple(yw), sig=f"ytt{tt}_{j}")
                c.do(V, lambda e, tt=tt, j=j: e.tensor_copy(
                        zall[:, j * 512:(j + 1) * 512], p4[:, 0:512]),
                     waits=(f"ytt{tt}_{j}",), sig=f"yte{tt}_{j}")

            def f_z2(e, tt=tt):
                e.tensor_tensor(zall[:, 2048:4096], zall[:, 0:2048], x1t[:, :],
                                ALU.add)
                e.tensor_copy(vj[:, 0:4], vj[:, 4:8])
                return e.reduce_sum(lns[:, 0:1], zall[:, 2048:4096], axis=AX.X)
            c.do(V, f_z2, waits=(f"yte{tt}_3", f"spr{tt}") +
                 ((f"o2n{tt - 1}",) if tt > 0 else ()), sig=f"z2{tt}")

            def f_sq2(e, tt=tt):
                e.activation(junk[:, :], zall[:, 2048:3072],
                             AF.Square, accum_out=lns[:, 2:3])
                return e.activation(junk[:, :], zall[:, 3072:4096],
                                    AF.Square, accum_out=lns[:, 3:4])
            c.do(Sc, f_sq2, waits=(f"z2{tt}",), sig=f"sq2_{tt}")

            def f_st2(e, tt=tt):
                e.tensor_copy(vj[:, 0:4], vj[:, 4:8])
                e.tensor_tensor(lns[:, 2:3], lns[:, 2:3], lns[:, 3:4], ALU.add)
                e.tensor_scalar(lns[:, 4:5], lns[:, 0:1], 1.0 / H, None, ALU.mult)
                e.tensor_scalar(lns[:, 5:6], lns[:, 2:3], 1.0 / H, None, ALU.mult)
                e.tensor_tensor(lns[:, 6:7], lns[:, 4:5], lns[:, 4:5], ALU.mult)
                e.tensor_copy(vj[:, 0:4], vj[:, 4:8])
                return e.tensor_tensor(lns[:, 5:6], lns[:, 5:6], lns[:, 6:7],
                                       ALU.subtract)
            c.do(V, f_st2, waits=(f"sq2_{tt}",), sig=f"lnsb{tt}")
            c.do(Sc, lambda e: e.activation(lns[:, 7:8], lns[:, 5:6], AF.Sqrt,
                                            bias=epsc[:, :]),
                 waits=(f"lnsb{tt}",), sig=f"sd2_{tt}")

            def f_o2(e, tt=tt):
                e.reciprocal(lns[:, 6:7], lns[:, 7:8])
                e.tensor_scalar(lns[:, 4:5], lns[:, 4:5], -1.0, None, ALU.mult)
                e.tensor_copy(vj[:, 0:4], vj[:, 4:8])
                return e.tensor_scalar(outt[:, :], zall[:, 2048:4096],
                                       lns[:, 4:5], lns[:, 6:7],
                                       ALU.add, ALU.mult)
            ow2 = [f"sd2_{tt}"]
            if tt > 0:
                ow2.append(f"outw{tt - 1}")
            c.do(V, f_o2, waits=tuple(ow2), sig=f"o2n{tt}")
            c.do(Y, lambda e, tt=tt: e.dma_start(d_out[tt * 128:(tt + 1) * 128, :],
                                                 outt[:, :]),
                 waits=(f"o2n{tt}",), sig=f"outw{tt}", k=16, sem="sout")
        c.do(Y, None, waits=("outw7",))

    walk(Ctx(counts, labels))
    handles = {}
    for name in list(counts):
        handles[name] = es.enter_context(nc.semaphore(f"sem_{name}"))

    spacers = {
        "vector": lambda e: e.tensor_copy(vj[:, 0:4], vj[:, 4:8]),
        "scalar": lambda e: e.copy(sjv[:, 0:1], sjv[:, 1:2]),
    }

    def mkctx(name, e):
        ctx = Ctx(counts, labels, handles, name, e)
        ctx.spacers = spacers
        return ctx

    with nc.Block() as block:
        block.gpsimd(lambda e: walk(mkctx("gpsimd", e)))
        block.sync(lambda e: walk(mkctx("sync", e)))
        block.tensor(lambda e: walk(mkctx("tensor", e)))
        block.vector(lambda e: walk(mkctx("vector", e)))
        block.scalar(lambda e: walk(mkctx("scalar", e)))

    es.pop_all()
    return nc


def _tile_stationary(wT, kc, mc):
    K, M = wT.shape
    return np.ascontiguousarray(
        wT.reshape(kc, 128, mc, 128).transpose(2, 1, 0, 3).reshape(mc * 128, K))


_PROG = [None]


def kernel(L0_attn_hidden_states, L0_attn_Wq, L0_attn_Wk, L0_attn_Wv, L0_attn_Wo,
           L0_ffn_gate_weight, L0_ffn_gate_up_weights, L0_ffn_down_weights,
           L0_ffn_shared_gate_up, L0_ffn_shared_down):
    x = np.asarray(L0_attn_hidden_states)
    wq = np.asarray(L0_attn_Wq)
    wk = np.asarray(L0_attn_Wk)
    wv = np.asarray(L0_attn_Wv)
    wo = np.asarray(L0_attn_Wo)
    gw = np.asarray(L0_ffn_gate_weight)
    guw = np.asarray(L0_ffn_gate_up_weights)
    dnw = np.asarray(L0_ffn_down_weights)
    sgu = np.asarray(L0_ffn_shared_gate_up)
    sdw = np.asarray(L0_ffn_shared_down)
    f4 = np.float32

    wq_t = _tile_stationary(np.ascontiguousarray(wq.T), 16, 16).astype(f4, copy=False)
    wk_t = _tile_stationary(np.ascontiguousarray(wk.T), 16, 4).astype(f4, copy=False)
    wv_t = _tile_stationary(np.ascontiguousarray(wv.T), 16, 4).astype(f4, copy=False)
    wo_t = np.ascontiguousarray(wo.T).astype(f4, copy=False)
    wg_t = np.ascontiguousarray(gw.T).astype(f4, copy=False)
    sgu_t = _tile_stationary(np.ascontiguousarray(sgu.T), 16, 44).astype(f4, copy=False)
    sd_t = _tile_stationary(np.ascontiguousarray(sdw.T), 22, 16).astype(f4, copy=False)
    gu_t = np.concatenate(
        [_tile_stationary(np.ascontiguousarray(guw[e].T), 16, 22) for e in range(E)],
        axis=0).astype(f4, copy=False)
    dn_t = np.concatenate(
        [_tile_stationary(np.ascontiguousarray(dnw[e].T), 11, 16) for e in range(E)],
        axis=0).astype(f4, copy=False)
    id_bf = np.eye(128, dtype=np.float32).astype(ml_dtypes.bfloat16)
    id_f32 = np.eye(128, dtype=np.float32)

    invf64 = 10000.0 ** (-np.arange(64) / 64.0)

    if _PROG[0] is None:
        _PROG[0] = build_program()
    nc = _PROG[0]

    in_maps = []
    for c in range(8):
        b, hh = c // 2, c % 2
        own = x[b, hh * TOK:(hh + 1) * TOK, :]
        oth = x[b, (1 - hh) * TOK:(2 - hh) * TOK, :]
        x_full = np.ascontiguousarray(np.concatenate([own, oth], axis=0)).astype(f4, copy=False)
        pos = np.concatenate([np.arange(TOK) + hh * TOK,
                              np.arange(TOK) + (1 - hh) * TOK]).astype(np.float64)
        f64 = pos[None, :] * invf64[:, None]
        ksin = np.rint(f64 / (2 * np.pi)).astype(f4)
        kcos = np.rint((f64 + np.pi / 2) / (2 * np.pi)).astype(f4)
        t0v = np.array([[float(hh * TOK)]], dtype=f4)
        in_maps.append({
            "x_full": x_full, "ksin": ksin, "kcos": kcos, "t0": t0v,
            "id_bf": id_bf, "id_f32": id_f32, "wk_t": wk_t, "wv_t": wv_t, "wq_t": wq_t,
            "wo_t": wo_t, "wg_t": wg_t, "sgu_t": sgu_t, "sd_t": sd_t,
            "gu_t": gu_t, "dn_t": dn_t,
        })

    res = run_bass_kernel_spmd(nc, in_maps, core_ids=list(range(8)))
    out = np.empty((4, S, H), dtype=np.float32)
    for c in range(8):
        b, hh = c // 2, c % 2
        out[b, hh * TOK:(hh + 1) * TOK, :] = res.results[c]["out"]
    return out
